# version 05
# baseline (speedup 1.0000x reference)
import numpy as np
import ml_dtypes
from contextlib import ExitStack

import concourse.bass as bass
import concourse.mybir as mybir
from concourse.bass_utils import run_bass_kernel_spmd

F32 = mybir.dt.float32
BF16 = mybir.dt.bfloat16
I32 = mybir.dt.int32
AF = mybir.ActivationFunctionType
ALU = mybir.AluOpType
AX = mybir.AxisListType

D = 4096
TOK = 1024
NT = 8
NK = 32
EPS = 1e-5
INV_D = 1.0 / D


class Sem:
    __slots__ = ("h", "n")

    def __init__(self, h):
        self.h = h
        self.n = 0

    def dma(self, instr):
        instr.then_inc(self.h, 16)
        self.n += 16
        return self.n

    def inc(self, instr):
        instr.then_inc(self.h, 1)
        self.n += 1
        return self.n


def build_nc():
    nc = bass.Bass("TRN2", target_bir_lowering=False)

    x_in = nc.dram_tensor("x_in", [TOK, D], F32, kind="ExternalInput").ap()
    mask = nc.dram_tensor("mask", [D, D], I32, kind="ExternalInput").ap()
    wt = nc.dram_tensor("wt", [D, D], F32, kind="ExternalInput").ap()
    qw = nc.dram_tensor("qw", [D, D], I32, kind="ExternalInput").ap()
    scl = nc.dram_tensor("scl", [128, 1024], F32, kind="ExternalInput").ap()
    zrl = nc.dram_tensor("zrl", [128, 1024], F32, kind="ExternalInput").ap()
    ab = nc.dram_tensor("ab", [1, D], F32, kind="ExternalInput").ap()
    fb = nc.dram_tensor("fb", [1, D], F32, kind="ExternalInput").ap()
    idn = nc.dram_tensor("idn", [128, 128], BF16, kind="ExternalInput").ap()
    nbsel_d = nc.dram_tensor("nbsel", [128, 256], BF16, kind="ExternalInput").ap()
    selpad_d = nc.dram_tensor("selpad", [128, D], BF16, kind="ExternalInput").ap()
    y = nc.dram_tensor("y", [TOK, D], F32, kind="ExternalOutput").ap()
    scr_f = nc.dram_tensor("scr_f", [8, NK, 128, 512], BF16, kind="Internal")
    scr_a = nc.dram_tensor("scr_a", [NK, 128, D], BF16, kind="Internal")

    with ExitStack() as st:
        E = st.enter_context
        R = [E(nc.sbuf_tensor(f"R{t}", [128, D], F32)) for t in range(NT)]
        scl_sb = E(nc.sbuf_tensor("scl_sb", [128, 1024], F32))
        zrl_sb = E(nc.sbuf_tensor("zrl_sb", [128, 1024], F32))
        biasA = E(nc.sbuf_tensor("biasA", [1, D], BF16))
        biasF = E(nc.sbuf_tensor("biasF", [1, D], BF16))
        onesb = E(nc.sbuf_tensor("onesb", [1, 128], BF16))
        idn_sb = E(nc.sbuf_tensor("idn_sb", [128, 128], BF16))
        nbsel_sb = E(nc.sbuf_tensor("nbsel_sb", [128, 256], BF16))
        bmT = E(nc.sbuf_tensor("bmT", [128, 128], BF16))
        istd = E(nc.sbuf_tensor("istd", [128, NT], F32))
        nmi = E(nc.sbuf_tensor("nmi", [128, NT], F32))
        stats6 = E(nc.sbuf_tensor("stats6", [128, 48], F32))
        mv = E(nc.sbuf_tensor("mv", [128, 2 * NT], F32))
        psA = E(nc.psum_tensor("psA", [128, 2048], F32))
        psB = E(nc.psum_tensor("psB", [128, 2048], F32))
        PS = [psA, psB]

        sQs = Sem(E(nc.semaphore("sQs")))
        sQp = Sem(E(nc.semaphore("sQp")))
        sQg_ = [Sem(E(nc.semaphore("sQg0"))), Sem(E(nc.semaphore("sQg1")))]
        sQw_ = [Sem(E(nc.semaphore("sQw0"))), Sem(E(nc.semaphore("sQw1")))]
        sWS = [Sem(E(nc.semaphore(f"sWS{i}"))) for i in range(8)]
        sP = [Sem(E(nc.semaphore(f"sP{i}"))) for i in range(3)]
        sT = Sem(E(nc.semaphore("sT")))
        sV = Sem(E(nc.semaphore("sV")))
        sA = Sem(E(nc.semaphore("sA")))

        def wg(eng, sem, val):
            if val > 0:
                eng.wait_ge(sem.h, val)

        r_load = [0] * NT
        ps_free = {(pi, b): 0 for pi in range(2) for b in range(4)}

        st0 = ExitStack()
        fb32 = st0.enter_context(nc.sbuf_tensor("fb32", [1, D], F32))
        ab32 = st0.enter_context(nc.sbuf_tensor("ab32", [1, D], F32))
        with nc.Block() as blk:
            @blk.sync
            def _(sync):
                for t in range(NT):
                    sQs.dma(sync.dma_start(out=R[t][:], in_=x_in[128 * t:128 * (t + 1), :]))
                    r_load[t] = sQs.n
                sQs.dma(sync.dma_start(out=idn_sb[:], in_=idn))
                sQs.dma(sync.dma_start(out=nbsel_sb[:], in_=nbsel_d))
                sQs.dma(sync.dma_start(out=fb32[:], in_=fb))
                sQs.dma(sync.dma_start(out=ab32[:], in_=ab))

            @blk.scalar
            def _(s):
                sQp.dma(s.dma_start(out=scl_sb[:], in_=scl))
                sQp.dma(s.dma_start(out=zrl_sb[:], in_=zrl))

            @blk.vector
            def _(v):
                wg(v, sQp, sQp.n)
                sV.inc(v.tensor_scalar_mul(scl_sb[:], scl_sb[:], 0.5))
                sV.inc(v.tensor_scalar_mul(zrl_sb[:], zrl_sb[:], 0.5))
                wg(v, sQs, sQs.n)
                sV.inc(v.tensor_scalar_mul(biasF[:], fb32[:], 0.5))
                sV.inc(v.tensor_copy(biasA[:], ab32[:]))
                sV.inc(v.memset(onesb[:], 1.0))
        b0_sqp = sQp.n
        b0_sqs = sQs.n
        st0.close()

        wb_f_done = [0] * NK
        with ExitStack() as st1:
            E1 = st1.enter_context
            qbuf = [E1(nc.sbuf_tensor(f"qb{i}", [128, 2048], I32)) for i in range(3)]
            w_oi = [E1(nc.sbuf_tensor(f"woi{i}", [128, D], BF16)) for i in range(2)]
            wS = [E1(nc.sbuf_tensor(f"wS{i}", [128, D], BF16)) for i in range(1)]

            q_load = [0] * (2 * NK)
            dq_half = [0] * (2 * NK)
            dq_done = [0] * NK
            dq_half_last = [0] * 3
            tr_done = {}
            ev_done = {}
            with nc.Block() as blk:
                NH = 2 * NK
                def emit_q_init(sync):
                    for h2 in range(3):
                        ot2, hf2 = h2 // 2, h2 % 2
                        sP[h2].dma(sync.dma_start(out=qbuf[h2][:],
                                   in_=qw[128 * ot2:128 * (ot2 + 1), 2048 * hf2:2048 * (hf2 + 1)]))
                        q_load[h2] = sP[h2].n
                blk.sync(emit_q_init)

                for ot in range(NK):
                    def emit_scalar(s, ot=ot):
                        if ot >= 2:
                            wg(s, sT, tr_done[(ot - 2, 7)])
                        for hf in (0, 1):
                            h = 2 * ot + hf
                            wg(s, sP[h % 3], q_load[h])
                            ins = None
                            for g in range(16):
                                c = 32 * ot + 16 * hf + g
                                ins = s.activation(w_oi[ot % 2][:, 2048 * hf + 128 * g:2048 * hf + 128 * (g + 1)],
                                                   qbuf[h % 3][:, 128 * g:128 * (g + 1)],
                                                   AF.Identity,
                                                   bias=zrl_sb[:, c:c + 1], scale=scl_sb[:, c:c + 1])
                            dq_half[h] = sA.inc(ins)
                        dq_done[ot] = dq_half[2 * ot + 1]
                        for hf in (0, 1):
                            h = 2 * ot + hf
                            dq_half_last[h % 3] = dq_half[h]
                    blk.scalar(emit_scalar)

                    def emit_q_pre(sync, ot=ot):
                        for hf in (0, 1):
                            h = 2 * ot + hf
                            if h + 3 < NH:
                                wg(sync, sA, dq_half[h])
                                nh = h + 3
                                not_, nhf = nh // 2, nh % 2
                                sP[h % 3].dma(sync.dma_start(
                                    out=qbuf[h % 3][:],
                                    in_=qw[128 * not_:128 * (not_ + 1), 2048 * nhf:2048 * (nhf + 1)]))
                                q_load[nh] = sP[h % 3].n
                    blk.sync(emit_q_pre)

                    def emit_tensor(t_, ot=ot):
                        wg(t_, sA, dq_done[ot])
                        for b in range(8):
                            pi, bank = b // 4, b % 4
                            wg(t_, sV, ps_free[(pi, bank)])
                            ins = None
                            for j in range(4):
                                blkidx = 4 * b + j
                                ins = t_.matmul(
                                    PS[pi][:, 512 * bank + 128 * j:512 * bank + 128 * (j + 1)],
                                    w_oi[ot % 2][:, 128 * blkidx:128 * (blkidx + 1)],
                                    idn_sb[:], start=True, stop=True)
                            tr_done[(ot, b)] = sT.inc(ins)
                    blk.tensor(emit_tensor)

                    def emit_vector(v, ot=ot):
                        if ot >= 1:
                            wg(v, sQw_[(ot - 1) % 2], wb_f_done[ot - 1])
                        for b in range(8):
                            pi, bank = b // 4, b % 4
                            wg(v, sT, tr_done[(ot, b)])
                            ins = v.tensor_copy(wS[0][:, 512 * b:512 * (b + 1)],
                                                PS[pi][:, 512 * bank:512 * (bank + 1)])
                            ev_done[(ot, b)] = sV.inc(ins)
                            ps_free[(pi, bank)] = ev_done[(ot, b)]
                    blk.vector(emit_vector)

                    def emit_gp(g, ot=ot):
                        wg(g, sV, ev_done[(ot, 7)])
                        og, oc0 = ot // 4, 128 * (ot % 4)
                        for spl in range(4):
                            it0 = 8 * spl
                            dst = bass.AP(scr_f, og * (NK * 128 * 512) + it0 * (128 * 512) + oc0,
                                          [[512, 128], [128 * 512, 8], [1, 128]])
                            src = bass.AP(wS[0], 128 * it0, [[D, 128], [128, 8], [1, 128]])
                            sQw_[ot % 2].dma(g.dma_start(out=dst, in_=src))
                        wb_f_done[ot] = sQw_[ot % 2].n
                    blk.gpsimd(emit_gp)

        wb_a_done = [0] * NK
        with ExitStack() as st2:
            E2 = st2.enter_context
            S1 = [E2(nc.sbuf_tensor(f"S1{i}", [128, 128], BF16)) for i in range(2)]
            bm_bf = E2(nc.sbuf_tensor("bm_bf", [128, 128], BF16))
            selpad_sb = E2(nc.sbuf_tensor("selpad_sb", [128, D], BF16))
            wm = [E2(nc.sbuf_tensor(f"wm{i}", [128, 2048], BF16)) for i in range(2)]
            bmsel = [E2(nc.sbuf_tensor(f"bms{i}", [128, 128], BF16)) for i in range(2)]
            st2a = ExitStack()
            mbuf = [st2a.enter_context(nc.sbuf_tensor(f"mb{i}", [128, 2048], I32)) for i in range(3)]

            m_load = [0] * (2 * NK)
            red_done = [0] * NK
            bmm_done = [0] * NK
            with nc.Block() as blk:
                def emit_selpad(s):
                    wg(s, sQw_[0], wb_f_done[NK - 2])
                    wg(s, sQw_[1], wb_f_done[NK - 1])
                    wg(s, sQp, sQp.n)
                    sQp.dma(s.dma_start(out=selpad_sb[:], in_=selpad_d))
                blk.scalar(emit_selpad)
                selpad_load = sQp.n

                NMH = 2 * NK
                def emit_mload0(sync):
                    wg(sync, sQw_[0], wb_f_done[NK - 2])
                    wg(sync, sQw_[1], wb_f_done[NK - 1])
                    for h2 in range(3):
                        nt2, hf2 = h2 // 2, h2 % 2
                        wg(sync, sA, dq_half_last[h2])
                        sP[h2].dma(sync.dma_start(out=mbuf[h2][:],
                                   in_=mask[128 * nt2:128 * (nt2 + 1), 2048 * hf2:2048 * (hf2 + 1)]))
                        m_load[h2] = sP[h2].n
                blk.sync(emit_mload0)

                for nt in range(NK):
                    def emit_vec(v, nt=nt):
                        if nt >= 2:
                            wg(v, sT, bmm_done[nt - 2])
                        ins = None
                        for hf in (0, 1):
                            h = 2 * nt + hf
                            wg(v, sP[h % 3], m_load[h])
                            with nc.allow_low_precision(
                                    reason="mask block sums: S1 <= 32 exact bf16; psum accum f32 exact"):
                                ins = v.tensor_reduce(S1[nt % 2][:, 64 * hf:64 * (hf + 1)],
                                                      mbuf[h % 3].rearrange("p (a b) -> p a b", b=32),
                                                      axis=AX.X, op=ALU.add)
                        red_done[nt] = sV.inc(ins)
                    blk.vector(emit_vec)

                    def emit_ten(t_, nt=nt):
                        wg(t_, sV, red_done[nt])
                        if nt == 0:
                            wg(t_, sV, ps_free[(0, 0)])
                        ins = t_.matmul(psA[:, 0:128],
                                        nbsel_sb[:, 128 - 4 * nt:256 - 4 * nt],
                                        S1[nt % 2][:], start=(nt == 0), stop=(nt == NK - 1))
                        bmm_done[nt] = sT.inc(ins)
                    blk.tensor(emit_ten)

                    def emit_mload(sync, nt=nt):
                        for hf in (0, 1):
                            h = 2 * nt + hf
                            if h + 3 < NMH:
                                wg(sync, sV, red_done[nt])
                                nh = h + 3
                                nnt, nhf = nh // 2, nh % 2
                                sP[h % 3].dma(sync.dma_start(
                                    out=mbuf[h % 3][:],
                                    in_=mask[128 * nnt:128 * (nnt + 1), 2048 * nhf:2048 * (nhf + 1)]))
                                m_load[nh] = sP[h % 3].n
                    blk.sync(emit_mload)

                def emit_thresh(v):
                    wg(v, sT, bmm_done[NK - 1])
                    sV.inc(v.tensor_single_scalar(bm_bf[:], psA[:, 0:128], 0.0, ALU.is_gt))
                blk.vector(emit_thresh)
                ps_free[(0, 0)] = sV.n

                def emit_bmt(t_):
                    wg(t_, sV, sV.n)
                    wg(t_, sV, ps_free[(1, 0)])
                    sT.inc(t_.matmul(psB[:, 0:128], bm_bf[:], idn_sb[:], start=True, stop=True))
                blk.tensor(emit_bmt)
                bmt_mm = sT.n

                def emit_bmt_ev(v):
                    wg(v, sT, bmt_mm)
                    sV.inc(v.tensor_copy(bmT[:], psB[:, 0:128]))
                blk.vector(emit_bmt_ev)
                bmt_done = sV.n
                ps_free[(1, 0)] = bmt_done

                NWH = 2 * NK
                st2a.close()
                st2d = ExitStack()
                wtb = [st2d.enter_context(nc.sbuf_tensor(f"wtb{i}", [128, 2048], F32)) for i in range(3)]
                wt_load = [0] * NWH
                sel_done = [0] * NK
                mul_done = [0] * NWH
                wb_a_half = [0] * NWH
                for kt in range(NK):
                    def emit_wt0(sync, kt=kt):
                        if kt == 0:
                            wg(sync, sV, red_done[NK - 1])
                            for h2 in range(3):
                                kt2, hf2 = h2 // 2, h2 % 2
                                sP[h2].dma(sync.dma_start(out=wtb[h2][:],
                                           in_=wt[128 * kt2:128 * (kt2 + 1), 2048 * hf2:2048 * (hf2 + 1)]))
                                wt_load[h2] = sP[h2].n
                    blk.sync(emit_wt0)

                    def emit_ten(t_, kt=kt):
                        if kt == 0:
                            wg(t_, sQp, selpad_load)
                            wg(t_, sV, bmt_done)
                        pi = kt % 2
                        wg(t_, sV, ps_free[(pi, 0)])
                        ins = t_.matmul(PS[pi][:, 0:128], selpad_sb[:, 128 * kt:128 * (kt + 1)],
                                        bmT[:], start=True, stop=True)
                        sel_done[kt] = sT.inc(ins)
                    blk.tensor(emit_ten)

                    def emit_vec(v, kt=kt):
                        pi = kt % 2
                        wg(v, sT, sel_done[kt])
                        ps_free[(pi, 0)] = sV.inc(v.tensor_copy(bmsel[pi][:], PS[pi][:, 0:128]))
                        wg(v, sV, ps_free[(pi, 0)])
                        for hf in (0, 1):
                            h = 2 * kt + hf
                            wg(v, sP[h % 3], wt_load[h])
                            if kt >= 1:
                                wg(v, sQp, wb_a_done[kt - 1])
                            bms_ap = bass.AP(bmsel[pi], 64 * hf,
                                             [[128, 128], [1, 64], [0, 32]])
                            ins = v.tensor_tensor(wm[h % 2].rearrange("p (a b) -> p a b", b=32),
                                                  wtb[h % 3].rearrange("p (a b) -> p a b", b=32),
                                                  bms_ap, ALU.mult)
                            mul_done[h] = sV.inc(ins)
                    blk.vector(emit_vec)

                    def emit_wtpre(sync, kt=kt):
                        for hf in (0, 1):
                            h = 2 * kt + hf
                            if h + 3 < NWH:
                                wg(sync, sV, mul_done[h])
                                nh = h + 3
                                nkt, nhf = nh // 2, nh % 2
                                sP[h % 3].dma(sync.dma_start(
                                    out=wtb[h % 3][:],
                                    in_=wt[128 * nkt:128 * (nkt + 1), 2048 * nhf:2048 * (nhf + 1)]))
                                wt_load[nh] = sP[h % 3].n
                    blk.sync(emit_wtpre)

                    def emit_sq(s, kt=kt):
                        for hf in (0, 1):
                            h = 2 * kt + hf
                            wg(s, sV, mul_done[h])
                            sQp.dma(s.dma_start(out=scr_a.ap()[kt][:, 2048 * hf:2048 * (hf + 1)],
                                                in_=wm[h % 2][:]))
                            wb_a_half[h] = sQp.n
                        wb_a_done[kt] = sQp.n
                    blk.scalar(emit_sq)

            st2d.close()

        st3 = ExitStack()
        E3 = st3.enter_context
        NWB = 6
        xnT = E3(nc.sbuf_tensor("xnT", [128, NK * 512], BF16))
        xnb = [E3(nc.sbuf_tensor(f"xnb{i}", [128, D], BF16)) for i in range(2)]
        wbuf = [E3(nc.sbuf_tensor(f"wbuf{i}", [128, 512], BF16)) for i in range(NWB)]
        evict_done = {}
        apply_done = [0] * NT
        trb_done = {}
        ssq_done = [0] * NT
        vv_done = [0] * NT
        istd_done = [0] * NT
        nmi_done = [0] * NT
        xnT_done = [0] * NT
        w_ready = [0] * (8 * NK)
        mm_done = [0] * 8
        ring_guard = [0] * 8
        sync_hwm = {"w0": 0, "w1": 0, "p": 0}
        order_hist = []

        def ln_stats(blk, sub, tt):
            def ln_v1(v, sub=sub, tt=tt):
                if sub == 0:
                    wg(v, sQs, b0_sqs)
                else:
                    wg(v, sV, evict_done[(sub - 1, tt)])
                for cc in range(8):
                    c = sV.inc(v.bn_stats(stats6[:, 6 * cc:6 * cc + 6],
                                          R[tt][:, 512 * cc:512 * (cc + 1)]))
                wg(v, sV, c)
                c = sV.inc(v.bn_aggr(mv[:, 2 * tt:2 * tt + 2], stats6[:]))
                wg(v, sV, c)
                c = sV.inc(v.tensor_scalar_add(mv[:, 2 * tt + 1:2 * tt + 2],
                                               mv[:, 2 * tt + 1:2 * tt + 2], EPS))
                wg(v, sV, c)
                vv_done[tt] = sV.inc(v.reciprocal(mv[:, 2 * tt + 1:2 * tt + 2],
                                                  mv[:, 2 * tt + 1:2 * tt + 2]))
            blk.vector(ln_v1)

            def ln_s(s, tt=tt):
                wg(s, sV, vv_done[tt])
                wg(s, sA, apply_done[tt])
                istd_done[tt] = sA.inc(s.activation(istd[:, tt:tt + 1],
                                                    mv[:, 2 * tt + 1:2 * tt + 2], AF.Sqrt))
            blk.scalar(ln_s)

            def ln_v2(v, tt=tt):
                wg(v, sA, istd_done[tt])
                ins = v.tensor_scalar(nmi[:, tt:tt + 1], mv[:, 2 * tt:2 * tt + 1],
                                      -1.0, istd[:, tt:tt + 1], ALU.mult, ALU.mult)
                nmi_done[tt] = sV.inc(ins)
            blk.vector(ln_v2)

        def emit_steps():
            with nc.Block() as blk:
                for tt in range(4):
                    ln_stats(blk, 0, tt)

                for s in range(6):
                    sub, half = s // 2, s % 2
                    is_ffn = (sub != 1)
                    bias_sb = biasF if is_ffn else biasA
                    tiles = [4 * half + i for i in range(4)]

                    for i, tt in enumerate(tiles):
                        xb = xnb[len(order_hist) % 2]

                        def emit_ap(s_, tt=tt, xb=xb):
                            if not order_hist:
                                wg(s_, sQp, wb_a_done[NK - 1])
                            wg(s_, sV, nmi_done[tt])
                            if len(order_hist) >= 2:
                                wg(s_, sT, trb_done[(order_hist[-2], 7)])
                            ins = s_.activation(xb[:], R[tt][:], AF.Identity,
                                                scale=istd[:, tt:tt + 1], bias=nmi[:, tt:tt + 1])
                            apply_done[tt] = sA.inc(ins)
                        blk.scalar(emit_ap)

                        def emit_tr(t_, tt=tt, xb=xb):
                            wg(t_, sA, apply_done[tt])
                            for b in range(8):
                                pi, bank = b // 4, b % 4
                                wg(t_, sV, ps_free[(pi, bank)])
                                ins = None
                                for j in range(4):
                                    blkidx = 4 * b + j
                                    ins = t_.matmul(
                                        PS[pi][:, 512 * bank + 128 * j:512 * bank + 128 * (j + 1)],
                                        xb[:, 128 * blkidx:128 * (blkidx + 1)],
                                        idn_sb[:], start=True, stop=True)
                                trb_done[(tt, b)] = sT.inc(ins)
                        blk.tensor(emit_tr)

                        def emit_xe(v, tt=tt, i=i):
                            for b in range(8):
                                pi, bank = b // 4, b % 4
                                wg(v, sT, trb_done[(tt, b)])
                                dst = bass.AP(xnT, 512 * (4 * b) + 128 * i,
                                              [[NK * 512, 128], [512, 4], [1, 128]])
                                srcp = PS[pi][:, 512 * bank:512 * (bank + 1)].rearrange(
                                    "p (a c) -> p a c", a=4)
                                ps_free[(pi, bank)] = sV.inc(v.tensor_copy(dst, srcp))
                            xnT_done[tt] = sV.n
                        blk.vector(emit_xe)
                        order_hist.append(tt)

                    if s + 1 < 6:
                        nsub, nhalf = (s + 1) // 2, (s + 1) % 2
                        for ntt in [4 * nhalf + i for i in range(4)]:
                            ln_stats(blk, nsub, ntt)

                    T0 = sT.n
                    for n8 in range(8):
                        def emit_sync(sync, n8=n8, T0=T0, sub=sub, is_ffn=is_ffn):
                            if sub == 0 and n8 == 0:
                                wg(sync, sQp, wb_a_done[NK - 1])
                            for k in range(NK):
                                idx = n8 * NK + k
                                slot = idx % NWB
                                wg(sync, sT, ring_guard[slot])
                                if is_ffn:
                                    lo = 4 * n8 + 2 if sub == 0 else NK - 2
                                    hi = 4 * n8 + 3 if sub == 0 else NK - 1
                                    if wb_f_done[lo] > sync_hwm["w0"]:
                                        sync.wait_ge(sQw_[0].h, wb_f_done[lo])
                                        sync_hwm["w0"] = wb_f_done[lo]
                                    if wb_f_done[hi] > sync_hwm["w1"]:
                                        sync.wait_ge(sQw_[1].h, wb_f_done[hi])
                                        sync_hwm["w1"] = wb_f_done[hi]
                                    srcd = scr_f.ap()[n8, k]
                                else:
                                    need = wb_a_done[k]
                                    if need > sync_hwm["p"]:
                                        sync.wait_ge(sQp.h, need)
                                        sync_hwm["p"] = need
                                    srcd = scr_a.ap()[k][:, 512 * n8:512 * (n8 + 1)]
                                sWS[slot].dma(sync.dma_start(out=wbuf[slot][:], in_=srcd))
                                w_ready[idx] = sWS[slot].n
                                ring_guard[slot] = T0 + 9 * n8 + (k // 4) + 1
                        blk.sync(emit_sync)

                        def emit_tensor(t_, n8=n8, T0=T0, half=half, bias_sb=bias_sb):
                            pi = n8 % 2
                            assert sT.n == T0 + 9 * n8, (sT.n, T0, n8)
                            for k in range(NK):
                                idx = n8 * NK + k
                                wg(t_, sWS[idx % NWB], w_ready[idx])
                                if k == 0:
                                    wg(t_, sV, max(ps_free[(pi, b)] for b in range(4)))
                                    wg(t_, sV, xnT_done[4 * half + 3])
                                ins = None
                                for t in range(4):
                                    ins = t_.matmul(PS[pi][:, 512 * t:512 * (t + 1)],
                                                    xnT[:, 512 * k + 128 * t:512 * k + 128 * (t + 1)],
                                                    wbuf[idx % NWB][:], start=(k == 0), stop=False)
                                if k % 4 == 3:
                                    sT.inc(ins)
                            ins = None
                            for t in range(4):
                                ins = t_.matmul(PS[pi][:, 512 * t:512 * (t + 1)], onesb[0:1, :],
                                                bias_sb[0:1, 512 * n8:512 * (n8 + 1)],
                                                start=False, stop=True)
                            mm_done[n8] = sT.inc(ins)
                            assert sT.n == T0 + 9 * (n8 + 1)
                        blk.tensor(emit_tensor)

                        def emit_vector(v, n8=n8, sub=sub, half=half):
                            pi = n8 % 2
                            wg(v, sT, mm_done[n8])
                            for t in range(4):
                                tt = 4 * half + t
                                ins = v.tensor_add(R[tt][:, 512 * n8:512 * (n8 + 1)],
                                                   R[tt][:, 512 * n8:512 * (n8 + 1)],
                                                   PS[pi][:, 512 * t:512 * (t + 1)])
                                c1 = sV.inc(ins)
                                ps_free[(pi, t)] = c1
                                if n8 == 7:
                                    evict_done[(sub, tt)] = c1
                        blk.vector(emit_vector)


        emit_steps()

        with nc.Block() as blk:
            @blk.sync
            def _(sync):
                for t in range(NT):
                    sync.wait_ge(sV.h, evict_done[(2, t)])
                    sQs.dma(sync.dma_start(out=y[128 * t:128 * (t + 1), :], in_=R[t][:]))
                sync.wait_ge(sQs.h, sQs.n)
        st3.close()

    return nc


_CACHE = {}


def _get_nc():
    if "nc" not in _CACHE:
        _CACHE["nc"] = build_nc()
    return _CACHE["nc"]


def kernel(L0_attn_data, L0_attn_mask, L0_attn_weight, L0_attn_bias,
           L0_ffn_qweight, L0_ffn_scales, L0_ffn_scaled_zeros, L0_ffn_bias):
    x = np.ascontiguousarray(np.asarray(L0_attn_data, dtype=np.float32).reshape(8192, D))
    mask_np = np.ascontiguousarray(np.asarray(L0_attn_mask, dtype=np.int32))
    wt_np = np.ascontiguousarray(np.asarray(L0_attn_weight, dtype=np.float32).T)
    ab_np = np.ascontiguousarray(np.asarray(L0_attn_bias, dtype=np.float32).reshape(1, D))
    qw_np = np.ascontiguousarray(np.asarray(L0_ffn_qweight, dtype=np.int32).reshape(D, D))
    scl_np = np.ascontiguousarray(np.asarray(L0_ffn_scales, dtype=np.float32)
                                  .reshape(NK, 128, 32).transpose(1, 0, 2).reshape(128, 1024))
    zrl_np = np.ascontiguousarray(np.asarray(L0_ffn_scaled_zeros, dtype=np.float32)
                                  .reshape(NK, 128, 32).transpose(1, 0, 2).reshape(128, 1024))
    fb_np = np.ascontiguousarray(np.asarray(L0_ffn_bias, dtype=np.float32).reshape(1, D))

    idn = np.eye(128).astype(ml_dtypes.bfloat16)
    nbsel = np.zeros((128, 256), dtype=ml_dtypes.bfloat16)
    for p in range(128):
        nbsel[p, 128 + p // 32] = 1
    selpad = np.zeros((128, D), dtype=ml_dtypes.bfloat16)
    for kb in range(128):
        selpad[kb, 32 * kb:32 * (kb + 1)] = 1

    common = {"mask": mask_np, "wt": wt_np, "qw": qw_np, "scl": scl_np, "zrl": zrl_np,
              "ab": ab_np, "fb": fb_np, "idn": idn, "nbsel": nbsel, "selpad": selpad}
    in_maps = [dict(common, x_in=np.ascontiguousarray(x[TOK * c:TOK * (c + 1)]))
               for c in range(8)]
    nc = _get_nc()
    res = run_bass_kernel_spmd(nc, in_maps, core_ids=list(range(8)))
    out = np.concatenate([np.asarray(res.results[c]["y"]) for c in range(8)], axis=0)
    return out.reshape(4, 2048, D).astype(np.float32)
